# version 21
# baseline (speedup 1.0000x reference)
import numpy as np
import ml_dtypes
from contextlib import ExitStack

import concourse.bass as bass
import concourse.mybir as mybir
from concourse.bass_utils import run_bass_kernel_spmd

F32 = mybir.dt.float32
BF16 = mybir.dt.bfloat16
AF = mybir.ActivationFunctionType
ALU = mybir.AluOpType
AX = mybir.AxisListType

D = 1024
T = 2048
NC = D // 128
TS = 512
NT = T // TS
NSCH = T // 128
HID = 512
HC = HID // 128
SHID = 2048
NEXP = 16
NE_ALL = NEXP + SHID // HID
H = 16
DH = 64
LN_EPS = 1e-5
Z_EPS = 1e-6
TOPK_BIG = 1e5

ENGINES = ("pe", "act", "dve", "pool", "sp")
BLOCK_ATTR = {"pe": "tensor", "act": "scalar", "dve": "vector", "pool": "gpsimd", "sp": "sync"}
DMA_STREAMS = ("dma_x0", "dma_x1", "dma_x2", "dma_x3", "dma_io", "dma_out", "dma_init",
               "dma_s0", "dma_s1", "dma_s2", "dma_s3", "dma_s4", "dma_s5")


class _Eng:
    def __init__(self, name):
        self.name = name
        self.prog = []
        self.count = 0
        self.waited = {}
        self.pending = []


class _Res:
    __slots__ = ("w", "readers")
    def __init__(self):
        self.w = None
        self.readers = {}


class Prog:

    def __init__(self):
        self.engs = {n: _Eng(n) for n in ENGINES}
        self.dma_counts = {q: 0 for q in DMA_STREAMS}
        self.res = {}
        self.n_ops = 0

    def _r(self, key):
        r = self.res.get(key)
        if r is None:
            r = self.res[key] = _Res()
        return r

    def _wait(self, eng, semkey, val):
        if eng.waited.get(semkey, -1) >= val:
            return
        eng.prog.append(("wait", semkey, val))
        eng.waited[semkey] = val

    def _edges(self, eng, reads, writes):
        need = {}
        for k in reads:
            r = self._r(k)
            if r.w is not None:
                sk, v = r.w
                need[sk] = max(need.get(sk, -1), v)
        for k in writes:
            for other in self.engs.values():
                if other is not eng:
                    for _, pk in other.pending:
                        if pk == k:
                            raise RuntimeError(
                                f"write of {k} races pending ops on {other.name}")
            r = self._r(k)
            if r.w is not None:
                sk, v = r.w
                need[sk] = max(need.get(sk, -1), v)
            for sk, v in r.readers.items():
                need[sk] = max(need.get(sk, -1), v)
        for sk, v in sorted(need.items()):
            if sk == "pe" and eng.name == "pe":
                continue
            self._wait(eng, sk, v)

    def op(self, engname, fn, reads=(), writes=(), inc=True):
        eng = self.engs[engname]
        self._edges(eng, reads, writes)
        self.n_ops += 1
        incs = []
        if inc:
            eng.count += 1
            val = eng.count
            incs = [(engname, 1)]
            for kind, k in eng.pending:
                r = self._r(k)
                if kind == "r":
                    r.readers[engname] = val
                else:
                    r.w = (engname, val)
                    r.readers = {}
            eng.pending = []
            for k in reads:
                self._r(k).readers[engname] = val
            for k in writes:
                r = self._r(k)
                r.w = (engname, val)
                r.readers = {}
        else:
            for k in reads:
                eng.pending.append(("r", k))
            for k in writes:
                eng.pending.append(("w", k))
        eng.prog.append(("call", fn, incs))

    def dma(self, queue, stream, fn, reads=(), writes=()):
        eng = self.engs[queue]
        semkey = stream
        self._edges(eng, reads, writes)
        self.n_ops += 1
        self.dma_counts[semkey] += 16
        val = self.dma_counts[semkey]
        eng.prog.append(("call", fn, [(semkey, 16)]))
        for k in reads:
            self._r(k).readers[semkey] = val
        for k in writes:
            r = self._r(k)
            r.w = (semkey, val)
            r.readers = {}

    def dma_batch(self, queue, stream, items):
        eng = self.engs[queue]
        semkey = stream
        for fn, reads, writes in items:
            self._edges(eng, reads, writes)
        final = self.dma_counts[semkey] + 16 * len(items)
        for fn, reads, writes in items:
            self.n_ops += 1
            self.dma_counts[semkey] += 16
            eng.prog.append(("call", fn, [(semkey, 16)]))
            for k in reads:
                self._r(k).readers[semkey] = final
            for k in writes:
                r = self._r(k)
                r.w = (semkey, final)
                r.readers = {}

    def finish(self):
        eng = self.engs["pool"]
        for q, v in sorted(self.dma_counts.items()):
            if v > 0:
                self._wait(eng, q, v)
        for n in ENGINES:
            if n != "pool" and self.engs[n].count > 0:
                self._wait(eng, n, self.engs[n].count)
        for n in ENGINES:
            assert not self.engs[n].pending, f"pending un-inc'd ops on {n}"

    def replay(self, block, sems):
        for name in ENGINES:
            eng = self.engs[name]
            if not eng.prog:
                continue
            def mk(prog):
                def run(e):
                    for item in prog:
                        if item[0] == "wait":
                            e.wait_ge(sems[item[1]], item[2])
                        else:
                            inst = item[1](e)
                            for semkey, n in item[2]:
                                inst.then_inc(sems[semkey], n)
                return run
            getattr(block, BLOCK_ATTR[name])(mk(eng.prog))



def build(n_experts=NE_ALL, do_f1=True, do_attn=True, do_f2=True):
    nc = bass.Bass("TRN2", target_bir_lowering=False)

    x_T = nc.dram_tensor("x_T", [D, T], F32, kind="ExternalInput")
    WqT = nc.dram_tensor("WqT", [D, D], F32, kind="ExternalInput")
    WkT = nc.dram_tensor("WkT", [D, D], F32, kind="ExternalInput")
    WvT = nc.dram_tensor("WvT", [D, D], F32, kind="ExternalInput")
    WoT = nc.dram_tensor("WoT", [D, D], F32, kind="ExternalInput")
    bq_c = nc.dram_tensor("bq_c", [128, NC], F32, kind="ExternalInput")
    bo_c = nc.dram_tensor("bo_c", [128, NC], F32, kind="ExternalInput")
    bk_r = nc.dram_tensor("bk_r", [1, D], F32, kind="ExternalInput")
    bv_r = nc.dram_tensor("bv_r", [1, D], F32, kind="ExternalInput")
    WgT = nc.dram_tensor("WgT", [D, 17], F32, kind="ExternalInput")
    WegT = nc.dram_tensor("WegT", [NEXP, D, HID], F32, kind="ExternalInput")
    WeuT = nc.dram_tensor("WeuT", [NEXP, D, HID], F32, kind="ExternalInput")
    WedT = nc.dram_tensor("WedT", [NEXP, HID, D], F32, kind="ExternalInput")
    WsgT = nc.dram_tensor("WsgT", [D, SHID], F32, kind="ExternalInput")
    WsuT = nc.dram_tensor("WsuT", [D, SHID], F32, kind="ExternalInput")
    WsdT = nc.dram_tensor("WsdT", [SHID, D], F32, kind="ExternalInput")
    out_T = nc.dram_tensor("out_T", [D, T], F32, kind="ExternalOutput")

    with ExitStack() as ctx:
        def sb(name, shape, dtype=F32):
            return ctx.enter_context(nc.sbuf_tensor(name, shape, dtype))
        x_sb = sb("x_sb", [128, NC, T])
        xn = sb("xn", [128, NC, T], BF16)
        slab = [sb(f"slab{i}", [128, 4096], BF16) for i in range(5)]
        h_w = sb("h_w", [128, 2, HC, TS], BF16)
        sg = sb("sg", [128, 2, TS], BF16)
        t1 = sb("t1", [128, 2, TS], BF16)
        t2 = sb("t2", [128, 2 * TS], BF16)
        w_sb = sb("w_sb", [128, TS])
        bsb = sb("bsb", [128, TS], BF16)
        scr32 = sb("scr32", [128, TS])
        t2f = scr32
        w_em = sb("w_em", [32, T], BF16)
        e_rows = sb("e_rows", [32, NE_ALL * 128], BF16)
        rows = sb("rows", [1, 3 * TS])
        rowsb = sb("rowsb", [1, 8 * TS], BF16)
        gsc = sb("gsc", [128, 4 * 48 + NSCH * 20])
        gcol = sb("gcol", [128, 4 * 6])
        wg_sb = sb("wg_sb", [128, NC, 17], BF16)
        bq_sb = sb("bq_sb", [128, NC])
        bo_sb = sb("bo_sb", [128, NC])
        nbq_sb = sb("nbq_sb", [128, NC])
        ones_bf = sb("ones_bf", [1, 128], BF16)
        ones_f = sb("ones_f", [1, 128], F32)
        onec_f = sb("onec_f", [128, 1], F32)
        onec_bf = sb("onec_bf", [128, 1], BF16)
        ident = sb("ident", [128, 128], F32)
        c_eps = sb("c_eps", [1, 1])
        Ksch = sb("Ksch", [128, 2, D], BF16)
        Vsch = sb("Vsch", [128, 2, H * 65], BF16)
        ksmat = sb("ksmat", [128, NC, 16], BF16)
        esel = sb("esel", [16, NC * 128], BF16)
        rbf = sb("rbf", [32, TS], BF16)
        KVacc = sb("KVacc", [128, 8 * 65])
        KVbf = sb("KVbf", [128, 8 * 65], BF16)
        qs = sb("qs", [128, NC, TS], BF16)
        Qp = qs
        stripe = qs
        bk_sb = qs[0:1, 0:2, :].rearrange("p a b -> p (a b)")
        bv_sb = qs[0:1, 2:4, :].rearrange("p a b -> p (a b)")

        pb = [ctx.enter_context(nc.psum_tensor(f"pb{i}", [128, TS], F32)) for i in range(8)]

        semkeys = list(ENGINES) + list(DMA_STREAMS)
        sems = {k: ctx.enter_context(nc.semaphore(f"sem_{k}")) for k in semkeys}

        P = Prog()
        block = ctx.enter_context(nc.Block())

        def R(j):
            return rows[:, j * TS:(j + 1) * TS]

        def slab_g(i, kc, col):
            return slab[i][:, kc * 512 + col * 128: kc * 512 + col * 128 + 128]

        def slab_d(i, kc, col):
            return slab[i][:, kc * 1024 + col * 128: kc * 1024 + col * 128 + 128]

        def xs(kc, t):
            return x_sb[:, kc, t * TS:(t + 1) * TS]

        def xns(kc, t):
            return xn[:, kc, t * TS:(t + 1) * TS]

        P.op("pool", lambda e: e.memset(ones_bf[:, :], 1.0), writes=[("ones",)])
        P.op("pool", lambda e: e.memset(ones_f[:, :], 1.0), writes=[("ones",)])
        P.op("pool", lambda e: e.memset(onec_f[:, :], 1.0), writes=[("ones",)])
        P.op("pool", lambda e: e.memset(onec_bf[:, :], 1.0), writes=[("ones",)])
        P.op("pool", lambda e: e.memset(ident[:, :], 0.0), writes=[("ident",)])
        P.op("pool", lambda e: e.affine_select(
            out=ident[:, :], in_=ident[:, :], compare_op=ALU.not_equal,
            fill=1.0, base=0, pattern=[[-1, 128]], channel_multiplier=1),
            reads=[("ident",)], writes=[("ident",)])
        for par in range(2):
            P.op("pool", lambda e, par=par: e.memset(
                bass.AP(Vsch, par * H * 65 + 64, [[2 * H * 65, 128], [65, H]]), 1.0),
                writes=[("vones",)])
        P.op("pool", lambda e: e.memset(c_eps[:, :], LN_EPS), writes=[("consts",)])
        P.op("pool", lambda e: e.memset(w_em[:, :], 0.0),
             writes=[("wem", sch) for sch in range(NSCH)])

        for t in range(NT):
            P.dma("sp" if t < 2 else "act", f"dma_x{t}", lambda e, t=t: e.dma_start(
                x_sb[:, :, t * TS:(t + 1) * TS],
                x_T[:, t * TS:(t + 1) * TS].rearrange("(c p) t -> p c t", p=128)),
                writes=[("x", fc, t) for fc in range(NC)])
        P.dma_batch("sp", "dma_io", [
            (lambda e: e.dma_start(bq_sb[:, :], bq_c[:, :]), [], [("bq",)]),
            (lambda e: e.dma_start(bo_sb[:, :], bo_c[:, :]), [], [("bo",)]),
        ])
        P.dma_batch("pool", "dma_init", [
            (lambda e: e.dma_start(bk_sb[:, :], bk_r[:, :]), [], [("bk",)]),
            (lambda e: e.dma_start(bv_sb[:, :], bv_r[:, :]), [], [("bv",)]),
            (lambda e: e.dma_start(
                wg_sb[:, :, :], WgT.rearrange("(c p) g -> p c g", p=128)), [], [("wg",)]),
        ])
        P.op("dve", lambda e: e.tensor_scalar_mul(nbq_sb[:, :], bq_sb[:, :], -1.0),
             reads=[("bq",)], writes=[("nbq",)])

        P.op("pool", lambda e: e.memset(e_rows[:, :], 0.0), writes=[("erows",)])
        P.op("pool", lambda e: e.affine_select(
            out=bass.AP(e_rows, 0, [[NE_ALL * 128, 32], [128, NE_ALL], [1, 128]]),
            in_=bass.AP(e_rows, 0, [[NE_ALL * 128, 32], [128, NE_ALL], [1, 128]]),
            compare_op=ALU.not_equal, fill=1.0, base=0,
            pattern=[[-1, NE_ALL], [0, 128]], channel_multiplier=1),
            reads=[("erows",)], writes=[("erows",)])
        P.op("pool", lambda e: e.memset(esel[:, :], 0.0), writes=[("esel",)])
        P.op("pool", lambda e: e.affine_select(
            out=bass.AP(esel, 0, [[NC * 128, 16], [128, NC], [64, 2], [1, 64]]),
            in_=bass.AP(esel, 0, [[NC * 128, 16], [128, NC], [64, 2], [1, 64]]),
            compare_op=ALU.not_equal, fill=1.0, base=0,
            pattern=[[-2, NC], [-1, 2], [0, 64]], channel_multiplier=1),
            reads=[("esel",)], writes=[("esel",)])

        def RB(j):
            return rowsb[:, j * TS:(j + 1) * TS]

        def plan_ln(tag):
            def stats(t):
                for kc in range(NC):
                    par = kc % 2
                    tk = ("t2a",) if par == 0 else ("t2b",)
                    sl = slice(0, TS) if par == 0 else slice(TS, 2 * TS)
                    P.op("act", lambda e, kc=kc, t=t, sl=sl: e.activation(
                        t2[:, sl], xs(kc, t), AF.Copy),
                        reads=[("x", kc, t)], writes=[tk])
                    P.op("pool", lambda e, kc=kc, sl=sl, par=par: e.tensor_mul(
                        t1[:, par, :], t2[:, sl], t2[:, sl]),
                        reads=[tk], writes=[("t1", par)])
                    P.op("pe", lambda e, kc=kc, t=t, sl=sl: e.matmul(
                        pb[t][0:1, :], onec_bf[:, :], t2[:, sl],
                        start=(kc == 0), stop=(kc == NC - 1)),
                        reads=[("ones",), tk], writes=[("pb", t)],
                        inc=True)
                    P.op("pe", lambda e, kc=kc, t=t, par=par: e.matmul(
                        pb[4 + t][0:1, :], onec_bf[:, :], t1[:, par, :],
                        start=(kc == 0), stop=(kc == NC - 1)),
                        reads=[("ones",), ("t1", par)], writes=[("pb", 4 + t)],
                        inc=True)

            def rowmath(t):
                pr = t % 4
                P.op("dve", lambda e, t=t: e.tensor_scalar_mul(R(0), pb[t][0:1, :], 1.0 / D),
                     reads=[("pb", t)], writes=[("rows", 0)])
                P.op("dve", lambda e, t=t: e.tensor_scalar_mul(R(1), pb[4 + t][0:1, :], 1.0 / D),
                     reads=[("pb", 4 + t)], writes=[("rows", 1)])
                P.op("dve", lambda e: e.tensor_mul(R(2), R(0), R(0)),
                     reads=[("rows", 0)], writes=[("rows", 2)])
                P.op("dve", lambda e: e.tensor_sub(R(1), R(1), R(2)),
                     reads=[("rows", 1), ("rows", 2)], writes=[("rows", 1)])
                P.op("act", lambda e: e.activation(R(2), R(1), AF.Sqrt, bias=c_eps[0:1, 0:1]),
                     reads=[("rows", 1), ("consts",)], writes=[("rows", 2)])
                P.op("dve", lambda e: e.reciprocal(R(1), R(2)),
                     reads=[("rows", 2)], writes=[("rows", 1)])
                P.op("dve", lambda e: e.scalar_tensor_tensor(
                    R(2), R(0), -1.0, R(1), op0=ALU.mult, op1=ALU.mult),
                    reads=[("rows", 0), ("rows", 1)], writes=[("rows", 2)])
                P.op("act", lambda e, pr=pr: e.activation(RB(2 * pr), R(1), AF.Copy),
                     reads=[("rows", 1)], writes=[("rowsb", 2 * pr)])
                P.op("act", lambda e, pr=pr: e.activation(RB(2 * pr + 1), R(2), AF.Copy),
                     reads=[("rows", 2)], writes=[("rowsb", 2 * pr + 1)])

            def bca_mm(t):
                pr = t % 4
                ba, bb = t, 4 + t
                P.op("pe", lambda e, pr=pr, ba=ba: e.matmul(
                    pb[ba][:, :], ones_bf[:, :], RB(2 * pr), start=True, stop=True),
                    reads=[("ones",), ("rowsb", 2 * pr)], writes=[("pb", ba)])
                P.op("pe", lambda e, pr=pr, bb=bb: e.matmul(
                    pb[bb][:, :], ones_bf[:, :], RB(2 * pr + 1), start=True, stop=True),
                    reads=[("ones",), ("rowsb", 2 * pr + 1)], writes=[("pb", bb)])

            def bca_xn(t):
                ba, bb = t, 4 + t
                P.op("act", lambda e, bb=bb: e.activation(bsb[:, :], pb[bb][:, :], AF.Copy),
                     reads=[("pb", bb)], writes=[("bsb",)])
                for kc in range(NC):
                    tk = ("t2a",) if kc % 2 == 0 else ("t2b",)
                    sl = slice(0, TS) if kc % 2 == 0 else slice(TS, 2 * TS)
                    P.op("dve", lambda e, kc=kc, t=t, ba=ba, sl=sl: e.tensor_mul(
                        t2[:, sl], xs(kc, t), pb[ba][:, :]),
                        reads=[("x", kc, t), ("pb", ba)], writes=[tk])
                    P.op("dve" if kc % 2 == 0 else "pool",
                         lambda e, kc=kc, t=t, sl=sl: e.tensor_add(
                             xns(kc, t), t2[:, sl], bsb[:, :]),
                         reads=[tk, ("bsb",)], writes=[("xn", kc, t)])

            stats(0)
            stats(1)
            rowmath(0)
            stats(2)
            rowmath(1)
            stats(3)
            rowmath(2)
            rowmath(3)
            return {"bca_mm": bca_mm, "bca_xn": bca_xn}

        def plan_gates(tag):
            def lbank(sch):
                return 6 + (sch // 4) % 2
            def lcol(sch, n=17):
                return (sch % 4) * 17, (sch % 4) * 17 + n
            L = lambda sch: pb[lbank(sch)][:, lcol(sch, 16)[0]: lcol(sch, 16)[1]]
            SG = lambda sch: pb[lbank(sch)][:, lcol(sch)[0] + 16: lcol(sch)[1]]
            A = lambda sch: gsc[:, (sch % 4) * 48: (sch % 4) * 48 + 16]
            B = lambda sch: gsc[:, (sch % 4) * 48 + 16: (sch % 4) * 48 + 32]
            C = lambda sch: gsc[:, (sch % 4) * 48 + 32: (sch % 4) * 48 + 48]
            W = lambda sch, j0, n: gsc[:, 192 + sch * 20 + j0: 192 + sch * 20 + j0 + n]
            g1 = lambda sch, j: gcol[:, (sch % 4) * 6 + j: (sch % 4) * 6 + j + 1]

            def logits(t):
                for sch in range(4 * t, 4 * t + 4):
                    for kc in range(NC):
                        P.op("pe", lambda e, kc=kc, sch=sch: e.matmul(
                            pb[lbank(sch)][:, lcol(sch)[0]: lcol(sch)[1]],
                            xn[:, kc, sch * 128:(sch + 1) * 128],
                            wg_sb[:, kc, :], start=(kc == 0), stop=(kc == NC - 1)),
                            reads=[("xn", kc, t), ("wg",)], writes=[("pb", lbank(sch))],
                            inc=(kc == NC - 1))

            def chain(t):
                schs = range(4 * t, 4 * t + 4)
                for sch in schs:
                    P.op("dve", lambda e, sch=sch: e.reduce_max(g1(sch, 0), L(sch), axis=AX.X),
                         reads=[("pb", lbank(sch))], writes=[("gk", sch % 4, 0)])
                for sch in schs:
                    P.op("dve", lambda e, sch=sch: e.tensor_scalar(
                        A(sch), L(sch), g1(sch, 0), None, op0=ALU.subtract),
                        reads=[("pb", lbank(sch)), ("gk", sch % 4, 0)], writes=[("ga", sch % 4)])
                for sch in schs:
                    P.op("pool", lambda e, sch=sch: e.tensor_single_scalar(
                        B(sch), A(sch), 0.0, op=ALU.is_ge),
                        reads=[("ga", sch % 4)], writes=[("gb", sch % 4)])
                for sch in schs:
                    P.op("dve", lambda e, sch=sch: e.scalar_tensor_tensor(
                        C(sch), B(sch), -TOPK_BIG, A(sch), op0=ALU.mult, op1=ALU.add),
                        reads=[("ga", sch % 4), ("gb", sch % 4)], writes=[("gc", sch % 4)])
                for sch in schs:
                    P.op("dve", lambda e, sch=sch: e.reduce_max(g1(sch, 1), C(sch), axis=AX.X),
                         reads=[("gc", sch % 4)], writes=[("gk", sch % 4, 1)])
                for sch in schs:
                    P.op("dve", lambda e, sch=sch: e.tensor_scalar(
                        B(sch), A(sch), g1(sch, 1), None, op0=ALU.is_ge),
                        reads=[("ga", sch % 4), ("gk", sch % 4, 1)], writes=[("gb", sch % 4)])
                for sch in schs:
                    P.op("act", lambda e, sch=sch: e.activation(C(sch), A(sch), AF.Exp),
                         reads=[("ga", sch % 4)], writes=[("gc", sch % 4)])
                for sch in schs:
                    P.op("pool", lambda e, sch=sch: e.tensor_mul(A(sch), C(sch), B(sch)),
                         reads=[("gc", sch % 4), ("gb", sch % 4)], writes=[("ga", sch % 4)])
                for sch in schs:
                    P.op("dve", lambda e, sch=sch: e.reduce_sum(g1(sch, 2), A(sch), axis=AX.X),
                         reads=[("ga", sch % 4)], writes=[("gk", sch % 4, 2)])
                for sch in schs:
                    P.op("dve", lambda e, sch=sch: e.reciprocal(g1(sch, 3), g1(sch, 2)),
                         reads=[("gk", sch % 4, 2)], writes=[("gk", sch % 4, 3)])
                for sch in schs:
                    P.op("dve", lambda e, sch=sch: e.tensor_scalar(
                        W(sch, 0, 16), A(sch), g1(sch, 3), 0.5, op0=ALU.mult, op1=ALU.mult),
                        reads=[("ga", sch % 4), ("gk", sch % 4, 3)], writes=[("gw", sch)])
                for sch in schs:
                    P.op("act", lambda e, sch=sch: e.activation(g1(sch, 4), SG(sch), AF.Sigmoid),
                         reads=[("pb", lbank(sch))], writes=[("gk", sch % 4, 4)])
                for sch in schs:
                    P.op("dve", lambda e, sch=sch: e.tensor_scalar_mul(g1(sch, 5), g1(sch, 4), 0.5),
                         reads=[("gk", sch % 4, 4)], writes=[("gk", sch % 4, 5)])
                for sch in schs:
                    P.op("pool", lambda e, sch=sch: e.tensor_copy(
                        W(sch, 16, 4), bass.AP(gcol, (sch % 4) * 6 + 5, [[gcol.shape[1], 128], [0, 4]])),
                        reads=[("gk", sch % 4, 5)], writes=[("gw", sch)])

            def transposes(t):
                bk = 6 + t % 2
                for sch in range(4 * t, 4 * t + 4):
                    P.op("pe", lambda e, sch=sch, bk=bk: e.transpose(
                        pb[bk][0:20, 0:128], W(sch, 0, 20), ident[:, :]),
                        reads=[("gw", sch), ("ident",)], writes=[("pb", bk)])
                    P.op("act", lambda e, sch=sch, bk=bk: e.activation(
                        w_em[0:20, sch * 128:(sch + 1) * 128], pb[bk][0:20, 0:128], AF.Copy),
                        reads=[("pb", bk)], writes=[("wem", sch)])

            return {"logits": logits, "chain": chain, "transposes": transposes}

        def expert_srcs(e_i):
            if e_i < NEXP:
                gsrc = WegT[e_i, :, :].rearrange("(kc p) h -> p kc h", p=128)
                usrc = WeuT[e_i, :, :].rearrange("(kc p) h -> p kc h", p=128)
                dsrc = WedT[e_i, :, :].rearrange("(kc p) o -> p kc o", p=128)
            else:
                j = e_i - NEXP
                gsrc = WsgT[:, j * 512:(j + 1) * 512].rearrange("(kc p) h -> p kc h", p=128)
                usrc = WsuT[:, j * 512:(j + 1) * 512].rearrange("(kc p) h -> p kc h", p=128)
                dsrc = WsdT[j * 512:(j + 1) * 512, :].rearrange("(kc p) o -> p kc o", p=128)
            return gsrc, usrc, dsrc

        def plan_dma_gup(e_i):
            gsrc, usrc, _ = expert_srcs(e_i)
            sg_i = e_i % 2
            su_i = 2 + e_i % 2
            gdst = slab[sg_i][:, :].rearrange("p (kc h) -> p kc h", h=512)
            udst = slab[su_i][:, :].rearrange("p (kc h) -> p kc h", h=512)
            P.dma("pool", f"dma_s{sg_i}", lambda e, a=gdst, b=gsrc: e.dma_start(a, b), writes=[("slab", sg_i)])
            P.dma("pool", f"dma_s{su_i}", lambda e, a=udst, b=usrc: e.dma_start(a, b), writes=[("slab", su_i)])

        def plan_dma_down(e_i):
            _, _, dsrc = expert_srcs(e_i)
            ddst = slab[4][:, :].rearrange("p (kc o) -> p kc o", o=1024)
            P.dma("pool", "dma_s4", lambda e, a=ddst, b=dsrc: e.dma_start(a, b), writes=[("slab", 4)])

        def plan_ffn(tag, lnf, gates, tail_dma=None):
            plan_dma_gup(0)
            plan_dma_down(0)
            if n_experts > 1:
                plan_dma_gup(1)
            pending_down = []

            def emit_down(e_d, t_d, par):
                for oc in range(NC):
                    bd = 4 + oc % 2
                    for kc in range(HC):
                        P.op("pe", lambda e, kc=kc, oc=oc, bd=bd, par=par: e.matmul(
                            pb[bd][:, :], slab_d(4, kc, oc), h_w[:, par, kc, :],
                            start=(kc == 0), stop=(kc == HC - 1)),
                            reads=[("slab", 4), ("hw", par, kc)], writes=[("pb", bd)],
                            inc=(kc == HC - 1))
                    P.op("dve", lambda e, oc=oc, t=t_d, bd=bd: e.tensor_add(
                        xs(oc, t), xs(oc, t), pb[bd][:, :]),
                        reads=[("pb", bd), ("x", oc, t_d)], writes=[("x", oc, t_d)])

            for e_i in range(n_experts):
                sg_i = e_i % 2
                su_i = 2 + e_i % 2
                for t in range(NT):
                    it = e_i * NT + t
                    par = it % 2
                    if e_i == 0:
                        lnf["bca_mm"](t)
                        lnf["bca_xn"](t)
                        gates["logits"](t)
                        gates["chain"](t)
                    for hc in range(HC):
                        bg = hc % 2
                        bu = 2 + hc % 2
                        for kc in range(NC):
                            P.op("pe", lambda e, sg_i=sg_i, kc=kc, hc=hc, t=t, bg=bg: e.matmul(
                                pb[bg][:, :], slab_g(sg_i, kc, hc), xns(kc, t),
                                start=(kc == 0), stop=(kc == NC - 1)),
                                reads=[("slab", sg_i), ("xn", kc, t)], writes=[("pb", bg)],
                                inc=(kc == NC - 1))
                        if hc == 0:
                            if e_i == 0:
                                gates["transposes"](t)
                            bw_b = 6 + (it + 1) % 2
                            P.op("pe", lambda e, e_i=e_i, t=t, bw_b=bw_b: e.matmul(
                                pb[bw_b][:, :], e_rows[0:32, e_i * 128:(e_i + 1) * 128],
                                w_em[0:32, t * TS:(t + 1) * TS], start=True, stop=True),
                                reads=[("erows",)] + [("wem", sch) for sch in range(t * 4, t * 4 + 4)],
                                writes=[("pb", bw_b)])
                            P.op("act", lambda e, bw_b=bw_b: e.activation(
                                w_sb[:, :], pb[bw_b][:, :], AF.Copy),
                                reads=[("pb", bw_b)], writes=[("wsb",)])
                        for kc in range(NC):
                            P.op("pe", lambda e, su_i=su_i, kc=kc, hc=hc, t=t, bu=bu: e.matmul(
                                pb[bu][:, :], slab_g(su_i, kc, hc), xns(kc, t),
                                start=(kc == 0), stop=(kc == NC - 1)),
                                reads=[("slab", su_i), ("xn", kc, t)], writes=[("pb", bu)],
                                inc=(kc == NC - 1))
                        P.op("act", lambda e, hc=hc, bg=bg: e.activation(
                            sg[:, hc % 2, :], pb[bg][:, :], AF.Silu),
                            reads=[("pb", bg)], writes=[("sg", hc % 2)])
                        P.op("dve", lambda e, hc=hc: e.tensor_mul(
                            t2f[:, :], sg[:, hc % 2, :], w_sb[:, :]),
                            reads=[("sg", hc % 2), ("wsb",)], writes=[("scr32",)])
                        P.op("dve", lambda e, hc=hc, bu=bu, par=par: e.tensor_mul(
                            h_w[:, par, hc, :], t2f[:, :], pb[bu][:, :]),
                            reads=[("scr32",), ("pb", bu)], writes=[("hw", par, hc)])
                    if pending_down:
                        emit_down(*pending_down.pop())
                        if t == 0 and e_i >= 1:
                            plan_dma_down(e_i)
                    pending_down.append((e_i, t, par))
                if e_i + 2 < n_experts:
                    plan_dma_gup(e_i + 2)
                if tail_dma is not None and e_i in tail_dma:
                    tail_dma[e_i]()
            emit_down(*pending_down.pop())

        def _wo_tile(ec, fc):
            return slab_g(1 if fc < 4 else 2, ec, fc % 4)

        def load_w2(dram, s0, s1):
            P.dma("pool", f"dma_s{s0}", lambda e: e.dma_start(
                slab[s0][:, :].rearrange("p (kc h) -> p kc h", h=512),
                dram[:, 0:512].rearrange("(kc p) h -> p kc h", p=128)),
                writes=[("slab", s0)])
            P.dma("pool", f"dma_s{s1}", lambda e: e.dma_start(
                slab[s1][:, :].rearrange("p (kc h) -> p kc h", h=512),
                dram[:, 512:1024].rearrange("(kc p) h -> p kc h", p=128)),
                writes=[("slab", s1)])

        def attn_dma_a():
            P.dma("pool", "dma_s0", lambda e: e.dma_start(
                slab[0][:, :].rearrange("p (kc h) -> p kc h", h=512),
                WkT[:, 0:512].rearrange("(kc p) h -> p kc h", p=128)),
                writes=[("slab", 0)])
            P.dma("pool", "dma_s2", lambda e: e.dma_start(
                slab[2][:, :].rearrange("p (kc h) -> p kc h", h=512),
                WvT[:, 0:512].rearrange("(kc p) h -> p kc h", p=128)),
                writes=[("slab", 2)])

        def attn_dma_b():
            P.dma("pool", "dma_s1", lambda e: e.dma_start(
                slab[1][:, :].rearrange("p (kc h) -> p kc h", h=512),
                WkT[:, 512:1024].rearrange("(kc p) h -> p kc h", p=128)),
                writes=[("slab", 1)])
            P.dma("pool", "dma_s3", lambda e: e.dma_start(
                slab[3][:, :].rearrange("p (kc h) -> p kc h", h=512),
                WvT[:, 512:1024].rearrange("(kc p) h -> p kc h", p=128)),
                writes=[("slab", 3)])

        def plan_attn(lnf):
            P.dma("pool", "dma_s4", lambda e: e.dma_start(
                slab[4][:, :].rearrange("p (kc h) -> p kc h", h=512),
                WqT[:, 0:512].rearrange("(kc p) h -> p kc h", p=128)),
                writes=[("slab", 4)])

            def proj_fm(sch):
                par = sch % 2
                t = (sch * 128) // TS
                for n in range(2):
                    bK, bV = n, 2 + n
                    for kc in range(NC):
                        P.op("pe", lambda e, kc=kc, sch=sch, n=n, bK=bK: e.matmul(
                            pb[bK][:, :], xn[:, kc, sch * 128:(sch + 1) * 128],
                            slab[n][:, kc * 512:(kc + 1) * 512],
                            start=(kc == 0), stop=False),
                            reads=[("xn", kc, t), ("slab", n)], writes=[("pb", bK)],
                            inc=False)
                    P.op("pe", lambda e, n=n, bK=bK: e.matmul(
                        pb[bK][:, :], ones_bf[:, :], bk_sb[:, n * 512:(n + 1) * 512],
                        start=False, stop=True),
                        reads=[("ones",), ("bk",)], writes=[("pb", bK)])
                    for kc in range(NC):
                        P.op("pe", lambda e, kc=kc, sch=sch, n=n, bV=bV: e.matmul(
                            pb[bV][:, :], xn[:, kc, sch * 128:(sch + 1) * 128],
                            slab[2 + n][:, kc * 512:(kc + 1) * 512],
                            start=(kc == 0), stop=False),
                            reads=[("xn", kc, t), ("slab", 2 + n)], writes=[("pb", bV)],
                            inc=False)
                    P.op("pe", lambda e, n=n, bV=bV: e.matmul(
                        pb[bV][:, :], ones_bf[:, :], bv_sb[:, n * 512:(n + 1) * 512],
                        start=False, stop=True),
                        reads=[("ones",), ("bv",)], writes=[("pb", bV)])
                    P.op("act", lambda e, bK=bK: e.activation(
                        t1[:, 0, :], pb[bK][:, :], AF.Relu),
                        reads=[("pb", bK)], writes=[("t1", 0)])
                    P.op("act", lambda e, bK=bK: e.activation(
                        t1[:, 1, :], pb[bK][:, :], AF.Exp),
                        reads=[("pb", bK)], writes=[("t1", 1)])
                    P.op("pool", lambda e: e.tensor_scalar_min(
                        t1[:, 1, :], t1[:, 1, :], 1.0),
                        reads=[("t1", 1)], writes=[("t1", 1)])
                    P.op("dve", lambda e, n=n, par=par: e.tensor_add(
                        Ksch[:, par, n * 512:(n + 1) * 512], t1[:, 0, :], t1[:, 1, :]),
                        reads=[("t1", 0), ("t1", 1)], writes=[("Ksch", par, n)])
                    P.op("act", lambda e, n=n, par=par, bV=bV: e.activation(
                        bass.AP(Vsch, par * (H * 65) + n * 8 * 65,
                                [[2 * H * 65, 128], [65, 8], [1, 64]]),
                        pb[bV][:, :], AF.Copy),
                        reads=[("pb", bV)], writes=[("Vsch", par, n)])

            def kv_mms(sch):
                par = sch % 2
                for h in range(H):
                    bank = 4 if h < 14 else 5
                    coff = (h // 2) * 65 if h < 14 else 0
                    poff = 64 * (h % 2)
                    P.op("pe", lambda e, h=h, par=par, bank=bank, coff=coff, poff=poff: e.matmul(
                        pb[bank][poff:poff + 64, coff:coff + 65],
                        Ksch[:, par, h * 64:(h + 1) * 64],
                        Vsch[:, par, h * 65:(h + 1) * 65],
                        start=True, stop=True),
                        reads=[("Ksch", par, h // 8), ("Vsch", par, h // 8), ("vones",)],
                        writes=[("pb", bank)],
                        inc=(h in (13, 15)))
                if sch == 0:
                    P.op("dve", lambda e: e.tensor_copy(KVacc[:, 0:455], pb[4][:, 0:455]),
                         reads=[("pb", 4)], writes=[("kvacc",)])
                    P.op("dve", lambda e: e.tensor_copy(KVacc[:, 455:520], pb[5][:, 0:65]),
                         reads=[("pb", 5)], writes=[("kvacc2",)])
                else:
                    P.op("dve", lambda e: e.tensor_add(
                        KVacc[:, 0:455], KVacc[:, 0:455], pb[4][:, 0:455]),
                        reads=[("pb", 4), ("kvacc",)], writes=[("kvacc",)])
                    P.op("dve", lambda e: e.tensor_add(
                        KVacc[:, 455:520], KVacc[:, 455:520], pb[5][:, 0:65]),
                        reads=[("pb", 5), ("kvacc2",)], writes=[("kvacc2",)])

            for sch in range(NSCH):
                if sch % 4 == 0:
                    lnf["bca_mm"](sch // 4)
                    lnf["bca_xn"](sch // 4)
                proj_fm(sch)
                if sch >= 1:
                    kv_mms(sch - 1)
            kv_mms(NSCH - 1)
            P.op("act", lambda e: e.activation(KVbf[:, :], KVacc[:, :], AF.Copy),
                 reads=[("kvacc",), ("kvacc2",)], writes=[("kvbf",)])

            P.dma("pool", "dma_s0", lambda e: e.dma_start(
                slab[0][:, :].rearrange("p (kc h) -> p kc h", h=512),
                WqT[:, 512:1024].rearrange("(kc p) h -> p kc h", p=128)),
                writes=[("slab", 0)])
            P.dma("pool", "dma_s1", lambda e: e.dma_start(
                slab[1][:, :].rearrange("p (kc h) -> p kc h", h=512),
                WoT[:, 0:512].rearrange("(kc p) h -> p kc h", p=128)),
                writes=[("slab", 1)])
            P.dma("pool", "dma_s2", lambda e: e.dma_start(
                slab[2][:, :].rearrange("p (kc h) -> p kc h", h=512),
                WoT[:, 512:1024].rearrange("(kc p) h -> p kc h", p=128)),
                writes=[("slab", 2)])

            def kv_ap(h, c0, cn):
                coff = (h // 2) * 65 if h < 14 else 455
                poff = 64 * (h % 2)
                return KVbf[poff:poff + 64, coff + c0:coff + c0 + cn]

            P.op("pool", lambda e: e.memset(ksmat[:, :, :], 0.0), writes=[("ksmat",)])
            for h in range(H):
                P.op("act", lambda e, h=h: e.activation(
                    ksmat[64 * (h % 2):64 * (h % 2) + 64, h // 2, h:h + 1],
                    kv_ap(h, 64, 1), AF.Copy),
                    reads=[("kvbf",)], writes=[("ksmat",)])

            def qproj(t, dn=False):
                for qc in range(NC):
                    bq_b = qc % 2
                    for kc in range(NC):
                        P.op("pe", lambda e, kc=kc, qc=qc, t=t, bq_b=bq_b: e.matmul(
                            pb[bq_b][:, :], slab_g(4 if qc < 4 else 0, kc, qc % 4), xns(kc, t),
                            start=(kc == 0), stop=(kc == NC - 1)),
                            reads=[("slab", 4 if qc < 4 else 0), ("xn", kc, t)],
                            writes=[("pb", bq_b)], inc=(kc == NC - 1))
                    P.op("act", lambda e, qc=qc, bq_b=bq_b: e.activation(
                        t1[:, 0, :], pb[bq_b][:, :], AF.Relu, bias=bq_sb[:, qc:qc + 1]),
                        reads=[("pb", bq_b), ("bq",)], writes=[("t1", 0)])
                    P.op("act", lambda e, qc=qc, bq_b=bq_b: e.activation(
                        t1[:, 1, :], pb[bq_b][:, :], AF.Exp, bias=bq_sb[:, qc:qc + 1]),
                        reads=[("pb", bq_b), ("bq",)], writes=[("t1", 1)])
                    P.op("pool", lambda e: e.tensor_scalar_min(
                        t1[:, 1, :], t1[:, 1, :], 1.0),
                        reads=[("t1", 1)], writes=[("t1", 1)])
                    bias_w = [("bk",)] if qc < 2 else ([("bv",)] if qc < 4 else [])
                    P.op("dve", lambda e, qc=qc: e.tensor_add(
                        Qp[:, qc, :], t1[:, 0, :], t1[:, 1, :]),
                        reads=[("t1", 0), ("t1", 1)],
                        writes=[("Qp", qc), ("stripe", qc)] + bias_w)
                    if dn and qc >= 1:
                        dn_mm(qc - 1)
                if dn:
                    dn_mm(NC - 1)

            def dn_mm(kc):
                P.op("pe", lambda e, kc=kc: e.matmul(
                    pb[6][0:16, :], ksmat[:, kc, :], Qp[:, kc, :],
                    start=(kc == 0), stop=(kc == NC - 1)),
                    reads=[("ksmat",), ("Qp", kc)], writes=[("pb", 6)],
                    inc=(kc == NC - 1))

            def recip_chain():
                P.op("dve", lambda e: e.tensor_scalar_add(
                    scr32[0:16, :], pb[6][0:16, :], Z_EPS),
                    reads=[("pb", 6)], writes=[("scr32",)])
                P.op("dve", lambda e: e.reciprocal(scr32[0:16, :], scr32[0:16, :]),
                     reads=[("scr32",)], writes=[("scr32",)])
                P.op("act", lambda e: e.activation(rbf[0:16, :], scr32[0:16, :], AF.Copy),
                     reads=[("scr32",)], writes=[("rbf",)])

            def qrv(kc):
                return Ksch[:, 0, (kc % 2) * TS:(kc % 2) * TS + TS]

            def rb_num_wo(t):
                def rb_mm(kc):
                    br = 6 + kc % 2
                    P.op("pe", lambda e, kc=kc, br=br: e.matmul(
                        pb[br][:, :], esel[0:16, kc * 128:(kc + 1) * 128],
                        rbf[0:16, :], start=True, stop=True),
                        reads=[("esel",), ("rbf",)], writes=[("pb", br)])
                rb_mm(0)
                for kc in range(NC):
                    if kc + 1 < NC:
                        rb_mm(kc + 1)
                    br = 6 + kc % 2
                    P.op("dve", lambda e, kc=kc, br=br: e.tensor_mul(
                        qrv(kc), Qp[:, kc, :], pb[br][:, :]),
                        reads=[("Qp", kc), ("pb", br)], writes=[("qr", kc % 2)])
                    bn = 2 + kc % 2
                    for j in range(2):
                        h = 2 * kc + j
                        P.op("pe", lambda e, h=h, j=j, kc=kc, bn=bn: e.matmul(
                            pb[bn][64 * j:64 * j + 64, :], kv_ap(h, 0, 64),
                            qrv(kc)[64 * j:64 * j + 64, :], start=True, stop=True),
                            reads=[("kvbf",), ("qr", kc % 2)], writes=[("pb", bn)],
                            inc=(j == 1))
                    P.op("act", lambda e, kc=kc, bn=bn: e.activation(
                        stripe[:, kc, :], pb[bn][:, :], AF.Copy),
                        reads=[("pb", bn)], writes=[("stripe", kc), ("Qp", kc)])
                for fc in range(NC):
                    bw = 4 + fc % 2
                    for ec in range(NC):
                        P.op("pe", lambda e, fc=fc, ec=ec, bw=bw: e.matmul(
                            pb[bw][:, :], _wo_tile(ec, fc), stripe[:, ec, :],
                            start=(ec == 0), stop=(ec == NC - 1)),
                            reads=[("slab", 1), ("slab", 2), ("stripe", ec)],
                            writes=[("pb", bw)], inc=(ec == NC - 1))
                    P.op("dve", lambda e, fc=fc, t=t, bw=bw: e.scalar_tensor_tensor(
                        xs(fc, t), pb[bw][:, :], bo_sb[:, fc:fc + 1], xs(fc, t),
                        op0=ALU.add, op1=ALU.add),
                        reads=[("pb", bw), ("bo",), ("x", fc, t)],
                        writes=[("x", fc, t)])

            for t in range(NT):
                qproj(t, dn=True)
                recip_chain()
                rb_num_wo(t)

        if do_f1:
            plan_ffn("f1", plan_ln("f1"), plan_gates("f1"),
                     tail_dma={NE_ALL - 2: attn_dma_a, NE_ALL - 1: attn_dma_b} if do_attn else None)
        if do_attn:
            plan_attn(plan_ln("attn"))
        if do_f2:
            plan_ffn("f2", plan_ln("f2"), plan_gates("f2"))

        for t in range(NT):
            P.dma("sp", f"dma_x{t}", lambda e, t=t: e.dma_start(
                out_T[:, t * TS:(t + 1) * TS].rearrange("(c p) t -> p c t", p=128),
                x_sb[:, :, t * TS:(t + 1) * TS]),
                reads=[("x", fc, t) for fc in range(NC)])
        P.finish()
        P.replay(block, sems)
    return nc



def _stage(inputs):
    f = np.float32

    def T2(a):
        return np.ascontiguousarray(np.asarray(a).astype(f).T)

    Wg = np.asarray(inputs["L0_ffn_Wg"], dtype=f)
    Wsg = np.asarray(inputs["L0_ffn_Wsg"], dtype=f)
    wg_cat = np.ascontiguousarray(np.concatenate([Wg, Wsg], 0).T)
    shared = {
        "WqT": T2(inputs["L0_attn_Wq"]),
        "WkT": T2(inputs["L0_attn_Wk"]),
        "WvT": T2(inputs["L0_attn_Wv"]),
        "WoT": T2(inputs["L0_attn_Wo"]),
        "bq_c": np.ascontiguousarray(np.asarray(inputs["L0_attn_bq"], dtype=f).reshape(NC, 128).T),
        "bo_c": np.ascontiguousarray(np.asarray(inputs["L0_attn_bo"], dtype=f).reshape(NC, 128).T),
        "bk_r": np.asarray(inputs["L0_attn_bk"], dtype=f).reshape(1, D).copy(),
        "bv_r": np.asarray(inputs["L0_attn_bv"], dtype=f).reshape(1, D).copy(),
        "WgT": wg_cat,
        "WegT": np.ascontiguousarray(np.asarray(inputs["L0_ffn_We_gate"], dtype=f).transpose(0, 2, 1)),
        "WeuT": np.ascontiguousarray(np.asarray(inputs["L0_ffn_We_up"], dtype=f).transpose(0, 2, 1)),
        "WedT": np.ascontiguousarray(np.asarray(inputs["L0_ffn_We_down"], dtype=f).transpose(0, 2, 1)),
        "WsgT": T2(inputs["L0_ffn_Ws_gate"]),
        "WsuT": T2(inputs["L0_ffn_Ws_up"]),
        "WsdT": T2(inputs["L0_ffn_Ws_down"]),
    }
    x = np.asarray(inputs["L0_attn_x"], dtype=f)
    in_maps = []
    for c in range(8):
        m = dict(shared)
        m["x_T"] = np.ascontiguousarray(x[c].T)
        in_maps.append(m)
    return in_maps


def kernel(**inputs):
    in_maps = _stage(inputs)
    nc = build()
    res = run_bass_kernel_spmd(nc, in_maps, core_ids=list(range(8)))
    out = np.stack([np.asarray(r["out_T"]).T for r in res.results], axis=0)
    return np.ascontiguousarray(out.astype(np.float32))






# version 22 (final)
# speedup vs baseline: 1.0903x; 1.0903x over previous
import numpy as np
import ml_dtypes
from contextlib import ExitStack

import concourse.bass as bass
import concourse.mybir as mybir
from concourse.bass_utils import run_bass_kernel_spmd

F32 = mybir.dt.float32
BF16 = mybir.dt.bfloat16
AF = mybir.ActivationFunctionType
ALU = mybir.AluOpType
AX = mybir.AxisListType

D = 1024
T = 2048
NC = D // 128
TS = 512
NT = T // TS
NSCH = T // 128
HID = 512
HC = HID // 128
SHID = 2048
NEXP = 16
NE_ALL = NEXP + SHID // HID
H = 16
DH = 64
LN_EPS = 1e-5
Z_EPS = 1e-6
TOPK_BIG = 1e5

ENGINES = ("pe", "act", "dve", "pool", "sp")
BLOCK_ATTR = {"pe": "tensor", "act": "scalar", "dve": "vector", "pool": "gpsimd", "sp": "sync"}
DMA_STREAMS = ("dma_x0", "dma_x1", "dma_x2", "dma_x3", "dma_io", "dma_out", "dma_init",
               "dma_s0", "dma_s1", "dma_s2", "dma_s3", "dma_s4", "dma_s5")


class _Eng:
    def __init__(self, name):
        self.name = name
        self.prog = []
        self.count = 0
        self.waited = {}
        self.pending = []


class _Res:
    __slots__ = ("w", "readers")
    def __init__(self):
        self.w = None
        self.readers = {}


class Prog:

    def __init__(self):
        self.engs = {n: _Eng(n) for n in ENGINES}
        self.dma_counts = {q: 0 for q in DMA_STREAMS}
        self.res = {}
        self.n_ops = 0

    def _r(self, key):
        r = self.res.get(key)
        if r is None:
            r = self.res[key] = _Res()
        return r

    def _wait(self, eng, semkey, val):
        if eng.waited.get(semkey, -1) >= val:
            return
        eng.prog.append(("wait", semkey, val))
        eng.waited[semkey] = val

    def _edges(self, eng, reads, writes):
        need = {}
        for k in reads:
            r = self._r(k)
            if r.w is not None:
                sk, v = r.w
                need[sk] = max(need.get(sk, -1), v)
        for k in writes:
            for other in self.engs.values():
                if other is not eng:
                    for _, pk in other.pending:
                        if pk == k:
                            raise RuntimeError(
                                f"write of {k} races pending ops on {other.name}")
            r = self._r(k)
            if r.w is not None:
                sk, v = r.w
                need[sk] = max(need.get(sk, -1), v)
            for sk, v in r.readers.items():
                need[sk] = max(need.get(sk, -1), v)
        for sk, v in sorted(need.items()):
            if sk == "pe" and eng.name == "pe":
                continue
            self._wait(eng, sk, v)

    def op(self, engname, fn, reads=(), writes=(), inc=True):
        eng = self.engs[engname]
        self._edges(eng, reads, writes)
        self.n_ops += 1
        incs = []
        if inc:
            eng.count += 1
            val = eng.count
            incs = [(engname, 1)]
            for kind, k in eng.pending:
                r = self._r(k)
                if kind == "r":
                    r.readers[engname] = val
                else:
                    r.w = (engname, val)
                    r.readers = {}
            eng.pending = []
            for k in reads:
                self._r(k).readers[engname] = val
            for k in writes:
                r = self._r(k)
                r.w = (engname, val)
                r.readers = {}
        else:
            for k in reads:
                eng.pending.append(("r", k))
            for k in writes:
                eng.pending.append(("w", k))
        eng.prog.append(("call", fn, incs))

    def dma(self, queue, stream, fn, reads=(), writes=()):
        eng = self.engs[queue]
        semkey = stream
        self._edges(eng, reads, writes)
        self.n_ops += 1
        self.dma_counts[semkey] += 16
        val = self.dma_counts[semkey]
        eng.prog.append(("call", fn, [(semkey, 16)]))
        for k in reads:
            self._r(k).readers[semkey] = val
        for k in writes:
            r = self._r(k)
            r.w = (semkey, val)
            r.readers = {}

    def dma_batch(self, queue, stream, items):
        eng = self.engs[queue]
        semkey = stream
        for fn, reads, writes in items:
            self._edges(eng, reads, writes)
        final = self.dma_counts[semkey] + 16 * len(items)
        for fn, reads, writes in items:
            self.n_ops += 1
            self.dma_counts[semkey] += 16
            eng.prog.append(("call", fn, [(semkey, 16)]))
            for k in reads:
                self._r(k).readers[semkey] = final
            for k in writes:
                r = self._r(k)
                r.w = (semkey, final)
                r.readers = {}

    def finish(self):
        eng = self.engs["pool"]
        for q, v in sorted(self.dma_counts.items()):
            if v > 0:
                self._wait(eng, q, v)
        for n in ENGINES:
            if n != "pool" and self.engs[n].count > 0:
                self._wait(eng, n, self.engs[n].count)
        for n in ENGINES:
            assert not self.engs[n].pending, f"pending un-inc'd ops on {n}"

    def replay(self, block, sems):
        for name in ENGINES:
            eng = self.engs[name]
            if not eng.prog:
                continue
            def mk(prog):
                def run(e):
                    for item in prog:
                        if item[0] == "wait":
                            e.wait_ge(sems[item[1]], item[2])
                        else:
                            inst = item[1](e)
                            for semkey, n in item[2]:
                                inst.then_inc(sems[semkey], n)
                return run
            getattr(block, BLOCK_ATTR[name])(mk(eng.prog))



def build(n_experts=NE_ALL, do_f1=True, do_attn=True, do_f2=True):
    nc = bass.Bass("TRN2", target_bir_lowering=False)

    x_T = nc.dram_tensor("x_T", [D, T], F32, kind="ExternalInput")
    WqT = nc.dram_tensor("WqT", [D, D], F32, kind="ExternalInput")
    WkT = nc.dram_tensor("WkT", [D, D], F32, kind="ExternalInput")
    WvT = nc.dram_tensor("WvT", [D, D], F32, kind="ExternalInput")
    WoT = nc.dram_tensor("WoT", [D, D], F32, kind="ExternalInput")
    bq_c = nc.dram_tensor("bq_c", [128, NC], F32, kind="ExternalInput")
    bo_c = nc.dram_tensor("bo_c", [128, NC], F32, kind="ExternalInput")
    bk_r = nc.dram_tensor("bk_r", [1, D], F32, kind="ExternalInput")
    bv_r = nc.dram_tensor("bv_r", [1, D], F32, kind="ExternalInput")
    WgT = nc.dram_tensor("WgT", [D, 17], F32, kind="ExternalInput")
    WegT = nc.dram_tensor("WegT", [NEXP, D, HID], F32, kind="ExternalInput")
    WeuT = nc.dram_tensor("WeuT", [NEXP, D, HID], F32, kind="ExternalInput")
    WedT = nc.dram_tensor("WedT", [NEXP, HID, D], F32, kind="ExternalInput")
    WsgT = nc.dram_tensor("WsgT", [D, SHID], F32, kind="ExternalInput")
    WsuT = nc.dram_tensor("WsuT", [D, SHID], F32, kind="ExternalInput")
    WsdT = nc.dram_tensor("WsdT", [SHID, D], F32, kind="ExternalInput")
    out_T = nc.dram_tensor("out_T", [D, T], F32, kind="ExternalOutput")

    with ExitStack() as ctx:
        def sb(name, shape, dtype=F32):
            return ctx.enter_context(nc.sbuf_tensor(name, shape, dtype))
        x_sb = sb("x_sb", [128, NC, T])
        xn = sb("xn", [128, NC, T], BF16)
        slab = [sb(f"slab{i}", [128, 4096], BF16) for i in range(5)]
        h_w = sb("h_w", [128, 2, HC, TS], BF16)
        sg = sb("sg", [128, 2, TS], BF16)
        t1 = sb("t1", [128, 2, TS], BF16)
        t2 = sb("t2", [128, 2 * TS], BF16)
        w_sb = sb("w_sb", [128, TS])
        bsb = sb("bsb", [128, TS], BF16)
        scr32 = sb("scr32", [128, TS])
        t2f = scr32
        w_em = sb("w_em", [32, T], BF16)
        e_rows = sb("e_rows", [32, NE_ALL * 128], BF16)
        rows = sb("rows", [1, 3 * TS])
        rowsb = sb("rowsb", [1, 8 * TS], BF16)
        gsc = sb("gsc", [128, 4 * 48 + NSCH * 20])
        gcol = sb("gcol", [128, 4 * 6])
        wg_sb = sb("wg_sb", [128, NC, 17], BF16)
        bq_sb = sb("bq_sb", [128, NC])
        bo_sb = sb("bo_sb", [128, NC])
        nbq_sb = sb("nbq_sb", [128, NC])
        ones_bf = sb("ones_bf", [1, 128], BF16)
        ones_f = sb("ones_f", [1, 128], F32)
        onec_f = sb("onec_f", [128, 1], F32)
        onec_bf = sb("onec_bf", [128, 1], BF16)
        ident = sb("ident", [128, 128], F32)
        c_eps = sb("c_eps", [1, 1])
        Ksch = sb("Ksch", [128, 2, D], BF16)
        Vsch = sb("Vsch", [128, 2, H * 65], BF16)
        ksmat = sb("ksmat", [128, NC, 16], BF16)
        esel = sb("esel", [16, NC * 128], BF16)
        rbf = sb("rbf", [32, TS], BF16)
        KVacc = sb("KVacc", [128, 8 * 65])
        KVbf = sb("KVbf", [128, 8 * 65], BF16)
        qs = sb("qs", [128, NC, TS], BF16)
        Qp = qs
        stripe = qs
        bk_sb = qs[0:1, 0:2, :].rearrange("p a b -> p (a b)")
        bv_sb = qs[0:1, 2:4, :].rearrange("p a b -> p (a b)")

        pb = [ctx.enter_context(nc.psum_tensor(f"pb{i}", [128, TS], F32)) for i in range(8)]

        semkeys = list(ENGINES) + list(DMA_STREAMS)
        sems = {k: ctx.enter_context(nc.semaphore(f"sem_{k}")) for k in semkeys}

        P = Prog()
        block = ctx.enter_context(nc.Block())

        def R(j):
            return rows[:, j * TS:(j + 1) * TS]

        def slab_g(i, kc, col):
            return slab[i][:, kc * 512 + col * 128: kc * 512 + col * 128 + 128]

        def slab_d(i, kc, col):
            return slab[i][:, kc * 1024 + col * 128: kc * 1024 + col * 128 + 128]

        def xs(kc, t):
            return x_sb[:, kc, t * TS:(t + 1) * TS]

        def xns(kc, t):
            return xn[:, kc, t * TS:(t + 1) * TS]

        P.op("pool", lambda e: e.memset(ones_bf[:, :], 1.0), writes=[("ones",)])
        P.op("pool", lambda e: e.memset(ones_f[:, :], 1.0), writes=[("ones",)])
        P.op("pool", lambda e: e.memset(onec_f[:, :], 1.0), writes=[("ones",)])
        P.op("pool", lambda e: e.memset(onec_bf[:, :], 1.0), writes=[("ones",)])
        P.op("pool", lambda e: e.memset(ident[:, :], 0.0), writes=[("ident",)])
        P.op("pool", lambda e: e.affine_select(
            out=ident[:, :], in_=ident[:, :], compare_op=ALU.not_equal,
            fill=1.0, base=0, pattern=[[-1, 128]], channel_multiplier=1),
            reads=[("ident",)], writes=[("ident",)])
        for par in range(2):
            P.op("pool", lambda e, par=par: e.memset(
                bass.AP(Vsch, par * H * 65 + 64, [[2 * H * 65, 128], [65, H]]), 1.0),
                writes=[("vones",)])
        P.op("pool", lambda e: e.memset(c_eps[:, :], LN_EPS), writes=[("consts",)])
        P.op("pool", lambda e: e.memset(w_em[:, :], 0.0),
             writes=[("wem", sch) for sch in range(NSCH)])

        for t in range(NT):
            P.dma("sp" if t < 2 else "act", f"dma_x{t}", lambda e, t=t: e.dma_start(
                x_sb[:, :, t * TS:(t + 1) * TS],
                x_T[:, t * TS:(t + 1) * TS].rearrange("(c p) t -> p c t", p=128)),
                writes=[("x", fc, t) for fc in range(NC)])
        P.dma_batch("sp", "dma_io", [
            (lambda e: e.dma_start(bq_sb[:, :], bq_c[:, :]), [], [("bq",)]),
            (lambda e: e.dma_start(bo_sb[:, :], bo_c[:, :]), [], [("bo",)]),
        ])
        P.dma_batch("pool", "dma_init", [
            (lambda e: e.dma_start(bk_sb[:, :], bk_r[:, :]), [], [("bk",)]),
            (lambda e: e.dma_start(bv_sb[:, :], bv_r[:, :]), [], [("bv",)]),
            (lambda e: e.dma_start(
                wg_sb[:, :, :], WgT.rearrange("(c p) g -> p c g", p=128)), [], [("wg",)]),
        ])
        P.op("dve", lambda e: e.tensor_scalar_mul(nbq_sb[:, :], bq_sb[:, :], -1.0),
             reads=[("bq",)], writes=[("nbq",)])

        P.op("pool", lambda e: e.memset(e_rows[:, :], 0.0), writes=[("erows",)])
        P.op("pool", lambda e: e.affine_select(
            out=bass.AP(e_rows, 0, [[NE_ALL * 128, 32], [128, NE_ALL], [1, 128]]),
            in_=bass.AP(e_rows, 0, [[NE_ALL * 128, 32], [128, NE_ALL], [1, 128]]),
            compare_op=ALU.not_equal, fill=1.0, base=0,
            pattern=[[-1, NE_ALL], [0, 128]], channel_multiplier=1),
            reads=[("erows",)], writes=[("erows",)])
        P.op("pool", lambda e: e.memset(esel[:, :], 0.0), writes=[("esel",)])
        P.op("pool", lambda e: e.affine_select(
            out=bass.AP(esel, 0, [[NC * 128, 16], [128, NC], [64, 2], [1, 64]]),
            in_=bass.AP(esel, 0, [[NC * 128, 16], [128, NC], [64, 2], [1, 64]]),
            compare_op=ALU.not_equal, fill=1.0, base=0,
            pattern=[[-2, NC], [-1, 2], [0, 64]], channel_multiplier=1),
            reads=[("esel",)], writes=[("esel",)])

        def RB(j):
            return rowsb[:, j * TS:(j + 1) * TS]

        def plan_ln(tag):
            def stats(t):
                for kc in range(NC):
                    par = kc % 2
                    tk = ("t2a",) if par == 0 else ("t2b",)
                    sl = slice(0, TS) if par == 0 else slice(TS, 2 * TS)
                    P.op("act", lambda e, kc=kc, t=t, sl=sl: e.activation(
                        t2[:, sl], xs(kc, t), AF.Copy),
                        reads=[("x", kc, t)], writes=[tk])
                    P.op("pool", lambda e, kc=kc, sl=sl, par=par: e.tensor_mul(
                        t1[:, par, :], t2[:, sl], t2[:, sl]),
                        reads=[tk], writes=[("t1", par)])
                    P.op("pe", lambda e, kc=kc, t=t, sl=sl: e.matmul(
                        pb[t][0:1, :], onec_bf[:, :], t2[:, sl],
                        start=(kc == 0), stop=(kc == NC - 1)),
                        reads=[("ones",), tk], writes=[("pb", t)],
                        inc=True)
                    P.op("pe", lambda e, kc=kc, t=t, par=par: e.matmul(
                        pb[4 + t][0:1, :], onec_bf[:, :], t1[:, par, :],
                        start=(kc == 0), stop=(kc == NC - 1)),
                        reads=[("ones",), ("t1", par)], writes=[("pb", 4 + t)],
                        inc=True)

            def rowmath(t):
                pr = t % 4
                P.op("dve", lambda e, t=t: e.tensor_scalar_mul(R(0), pb[t][0:1, :], 1.0 / D),
                     reads=[("pb", t)], writes=[("rows", 0)])
                P.op("dve", lambda e, t=t: e.tensor_scalar_mul(R(1), pb[4 + t][0:1, :], 1.0 / D),
                     reads=[("pb", 4 + t)], writes=[("rows", 1)])
                P.op("dve", lambda e: e.tensor_mul(R(2), R(0), R(0)),
                     reads=[("rows", 0)], writes=[("rows", 2)])
                P.op("dve", lambda e: e.tensor_sub(R(1), R(1), R(2)),
                     reads=[("rows", 1), ("rows", 2)], writes=[("rows", 1)])
                P.op("act", lambda e: e.activation(R(2), R(1), AF.Sqrt, bias=c_eps[0:1, 0:1]),
                     reads=[("rows", 1), ("consts",)], writes=[("rows", 2)])
                P.op("dve", lambda e: e.reciprocal(R(1), R(2)),
                     reads=[("rows", 2)], writes=[("rows", 1)])
                P.op("dve", lambda e: e.scalar_tensor_tensor(
                    R(2), R(0), -1.0, R(1), op0=ALU.mult, op1=ALU.mult),
                    reads=[("rows", 0), ("rows", 1)], writes=[("rows", 2)])
                P.op("act", lambda e, pr=pr: e.activation(RB(2 * pr), R(1), AF.Copy),
                     reads=[("rows", 1)], writes=[("rowsb", 2 * pr)])
                P.op("act", lambda e, pr=pr: e.activation(RB(2 * pr + 1), R(2), AF.Copy),
                     reads=[("rows", 2)], writes=[("rowsb", 2 * pr + 1)])

            def bca_mm(t):
                pr = t % 4
                ba, bb = t, 4 + t
                P.op("pe", lambda e, pr=pr, ba=ba: e.matmul(
                    pb[ba][:, :], ones_bf[:, :], RB(2 * pr), start=True, stop=True),
                    reads=[("ones",), ("rowsb", 2 * pr)], writes=[("pb", ba)])
                P.op("pe", lambda e, pr=pr, bb=bb: e.matmul(
                    pb[bb][:, :], ones_bf[:, :], RB(2 * pr + 1), start=True, stop=True),
                    reads=[("ones",), ("rowsb", 2 * pr + 1)], writes=[("pb", bb)])

            def bca_xn(t):
                ba, bb = t, 4 + t
                P.op("act", lambda e, bb=bb: e.activation(bsb[:, :], pb[bb][:, :], AF.Copy),
                     reads=[("pb", bb)], writes=[("bsb",)])
                for kc in range(NC):
                    tk = ("t2a",) if kc % 2 == 0 else ("t2b",)
                    sl = slice(0, TS) if kc % 2 == 0 else slice(TS, 2 * TS)
                    P.op("dve", lambda e, kc=kc, t=t, ba=ba, sl=sl: e.tensor_mul(
                        t2[:, sl], xs(kc, t), pb[ba][:, :]),
                        reads=[("x", kc, t), ("pb", ba)], writes=[tk])
                    P.op("dve" if kc % 2 == 0 else "pool",
                         lambda e, kc=kc, t=t, sl=sl: e.tensor_add(
                             xns(kc, t), t2[:, sl], bsb[:, :]),
                         reads=[tk, ("bsb",)], writes=[("xn", kc, t)])

            stats(0)
            stats(1)
            rowmath(0)
            stats(2)
            rowmath(1)
            stats(3)
            rowmath(2)
            rowmath(3)
            return {"bca_mm": bca_mm, "bca_xn": bca_xn}

        def plan_gates(tag):
            def lbank(sch):
                return 6 + (sch // 4) % 2
            def lcol(sch, n=17):
                return (sch % 4) * 17, (sch % 4) * 17 + n
            L = lambda sch: pb[lbank(sch)][:, lcol(sch, 16)[0]: lcol(sch, 16)[1]]
            SG = lambda sch: pb[lbank(sch)][:, lcol(sch)[0] + 16: lcol(sch)[1]]
            A = lambda sch: gsc[:, (sch % 4) * 48: (sch % 4) * 48 + 16]
            B = lambda sch: gsc[:, (sch % 4) * 48 + 16: (sch % 4) * 48 + 32]
            C = lambda sch: gsc[:, (sch % 4) * 48 + 32: (sch % 4) * 48 + 48]
            W = lambda sch, j0, n: gsc[:, 192 + sch * 20 + j0: 192 + sch * 20 + j0 + n]
            g1 = lambda sch, j: gcol[:, (sch % 4) * 6 + j: (sch % 4) * 6 + j + 1]

            def logits(t):
                for sch in range(4 * t, 4 * t + 4):
                    for kc in range(NC):
                        P.op("pe", lambda e, kc=kc, sch=sch: e.matmul(
                            pb[lbank(sch)][:, lcol(sch)[0]: lcol(sch)[1]],
                            xn[:, kc, sch * 128:(sch + 1) * 128],
                            wg_sb[:, kc, :], start=(kc == 0), stop=(kc == NC - 1)),
                            reads=[("xn", kc, t), ("wg",)], writes=[("pb", lbank(sch))],
                            inc=(kc == NC - 1))

            def chain(t):
                schs = range(4 * t, 4 * t + 4)
                for sch in schs:
                    P.op("dve", lambda e, sch=sch: e.reduce_max(g1(sch, 0), L(sch), axis=AX.X),
                         reads=[("pb", lbank(sch))], writes=[("gk", sch % 4, 0)])
                for sch in schs:
                    P.op("dve", lambda e, sch=sch: e.tensor_scalar(
                        A(sch), L(sch), g1(sch, 0), None, op0=ALU.subtract),
                        reads=[("pb", lbank(sch)), ("gk", sch % 4, 0)], writes=[("ga", sch % 4)])
                for sch in schs:
                    P.op("pool", lambda e, sch=sch: e.tensor_single_scalar(
                        B(sch), A(sch), 0.0, op=ALU.is_ge),
                        reads=[("ga", sch % 4)], writes=[("gb", sch % 4)])
                for sch in schs:
                    P.op("dve", lambda e, sch=sch: e.scalar_tensor_tensor(
                        C(sch), B(sch), -TOPK_BIG, A(sch), op0=ALU.mult, op1=ALU.add),
                        reads=[("ga", sch % 4), ("gb", sch % 4)], writes=[("gc", sch % 4)])
                for sch in schs:
                    P.op("dve", lambda e, sch=sch: e.reduce_max(g1(sch, 1), C(sch), axis=AX.X),
                         reads=[("gc", sch % 4)], writes=[("gk", sch % 4, 1)])
                for sch in schs:
                    P.op("dve", lambda e, sch=sch: e.tensor_scalar(
                        B(sch), A(sch), g1(sch, 1), None, op0=ALU.is_ge),
                        reads=[("ga", sch % 4), ("gk", sch % 4, 1)], writes=[("gb", sch % 4)])
                for sch in schs:
                    P.op("act", lambda e, sch=sch: e.activation(C(sch), A(sch), AF.Exp),
                         reads=[("ga", sch % 4)], writes=[("gc", sch % 4)])
                for sch in schs:
                    P.op("pool", lambda e, sch=sch: e.tensor_mul(A(sch), C(sch), B(sch)),
                         reads=[("gc", sch % 4), ("gb", sch % 4)], writes=[("ga", sch % 4)])
                for sch in schs:
                    P.op("dve", lambda e, sch=sch: e.reduce_sum(g1(sch, 2), A(sch), axis=AX.X),
                         reads=[("ga", sch % 4)], writes=[("gk", sch % 4, 2)])
                for sch in schs:
                    P.op("dve", lambda e, sch=sch: e.reciprocal(g1(sch, 3), g1(sch, 2)),
                         reads=[("gk", sch % 4, 2)], writes=[("gk", sch % 4, 3)])
                for sch in schs:
                    P.op("dve", lambda e, sch=sch: e.tensor_scalar(
                        W(sch, 0, 16), A(sch), g1(sch, 3), 0.5, op0=ALU.mult, op1=ALU.mult),
                        reads=[("ga", sch % 4), ("gk", sch % 4, 3)], writes=[("gw", sch)])
                for sch in schs:
                    P.op("act", lambda e, sch=sch: e.activation(g1(sch, 4), SG(sch), AF.Sigmoid),
                         reads=[("pb", lbank(sch))], writes=[("gk", sch % 4, 4)])
                for sch in schs:
                    P.op("dve", lambda e, sch=sch: e.tensor_scalar_mul(g1(sch, 5), g1(sch, 4), 0.5),
                         reads=[("gk", sch % 4, 4)], writes=[("gk", sch % 4, 5)])
                for sch in schs:
                    P.op("pool", lambda e, sch=sch: e.tensor_copy(
                        W(sch, 16, 4), bass.AP(gcol, (sch % 4) * 6 + 5, [[gcol.shape[1], 128], [0, 4]])),
                        reads=[("gk", sch % 4, 5)], writes=[("gw", sch)])

            def transposes(t):
                bk = 6 + t % 2
                for sch in range(4 * t, 4 * t + 4):
                    P.op("pe", lambda e, sch=sch, bk=bk: e.transpose(
                        pb[bk][0:20, 0:128], W(sch, 0, 20), ident[:, :]),
                        reads=[("gw", sch), ("ident",)], writes=[("pb", bk)])
                    P.op("act", lambda e, sch=sch, bk=bk: e.activation(
                        w_em[0:20, sch * 128:(sch + 1) * 128], pb[bk][0:20, 0:128], AF.Copy),
                        reads=[("pb", bk)], writes=[("wem", sch)])

            return {"logits": logits, "chain": chain, "transposes": transposes}

        def expert_srcs(e_i):
            if e_i < NEXP:
                gsrc = WegT[e_i, :, :].rearrange("(kc p) h -> p kc h", p=128)
                usrc = WeuT[e_i, :, :].rearrange("(kc p) h -> p kc h", p=128)
                dsrc = WedT[e_i, :, :].rearrange("(kc p) o -> p kc o", p=128)
            else:
                j = e_i - NEXP
                gsrc = WsgT[:, j * 512:(j + 1) * 512].rearrange("(kc p) h -> p kc h", p=128)
                usrc = WsuT[:, j * 512:(j + 1) * 512].rearrange("(kc p) h -> p kc h", p=128)
                dsrc = WsdT[j * 512:(j + 1) * 512, :].rearrange("(kc p) o -> p kc o", p=128)
            return gsrc, usrc, dsrc

        def plan_dma_gup(e_i):
            gsrc, usrc, _ = expert_srcs(e_i)
            sg_i = e_i % 2
            su_i = 2 + e_i % 2
            gdst = slab[sg_i][:, :].rearrange("p (kc h) -> p kc h", h=512)
            udst = slab[su_i][:, :].rearrange("p (kc h) -> p kc h", h=512)
            P.dma("pool", f"dma_s{sg_i}", lambda e, a=gdst, b=gsrc: e.dma_start(a, b), writes=[("slab", sg_i)])
            P.dma("pool", f"dma_s{su_i}", lambda e, a=udst, b=usrc: e.dma_start(a, b), writes=[("slab", su_i)])

        def plan_dma_down(e_i):
            _, _, dsrc = expert_srcs(e_i)
            ddst = slab[4][:, :].rearrange("p (kc o) -> p kc o", o=1024)
            P.dma("pool", "dma_s4", lambda e, a=ddst, b=dsrc: e.dma_start(a, b), writes=[("slab", 4)])

        def plan_ffn(tag, lnf, gates, tail_dma=None):
            plan_dma_gup(0)
            plan_dma_down(0)
            if n_experts > 1:
                plan_dma_gup(1)
            pending_down = []

            def emit_down(e_d, t_d, par):
                for oc in range(NC):
                    bd = 4 + oc % 2
                    for kc in range(HC):
                        P.op("pe", lambda e, kc=kc, oc=oc, bd=bd, par=par: e.matmul(
                            pb[bd][:, :], slab_d(4, kc, oc), h_w[:, par, kc, :],
                            start=(kc == 0), stop=(kc == HC - 1)),
                            reads=[("slab", 4), ("hw", par, kc)], writes=[("pb", bd)],
                            inc=(kc == HC - 1))
                    P.op("dve", lambda e, oc=oc, t=t_d, bd=bd: e.tensor_add(
                        xs(oc, t), xs(oc, t), pb[bd][:, :]),
                        reads=[("pb", bd), ("x", oc, t_d)], writes=[("x", oc, t_d)])

            for e_i in range(n_experts):
                sg_i = e_i % 2
                su_i = 2 + e_i % 2
                for t in range(NT):
                    it = e_i * NT + t
                    par = it % 2
                    if e_i == 0:
                        lnf["bca_mm"](t)
                        lnf["bca_xn"](t)
                        gates["logits"](t)
                        gates["chain"](t)
                    for hc in range(HC):
                        bg = hc % 2
                        bu = 2 + hc % 2
                        for kc in range(NC):
                            P.op("pe", lambda e, sg_i=sg_i, kc=kc, hc=hc, t=t, bg=bg: e.matmul(
                                pb[bg][:, :], slab_g(sg_i, kc, hc), xns(kc, t),
                                start=(kc == 0), stop=(kc == NC - 1)),
                                reads=[("slab", sg_i), ("xn", kc, t)], writes=[("pb", bg)],
                                inc=(kc == NC - 1))
                        if hc == 0:
                            if e_i == 0:
                                gates["transposes"](t)
                            bw_b = 6 + (it + 1) % 2
                            P.op("pe", lambda e, e_i=e_i, t=t, bw_b=bw_b: e.matmul(
                                pb[bw_b][:, :], e_rows[0:32, e_i * 128:(e_i + 1) * 128],
                                w_em[0:32, t * TS:(t + 1) * TS], start=True, stop=True),
                                reads=[("erows",)] + [("wem", sch) for sch in range(t * 4, t * 4 + 4)],
                                writes=[("pb", bw_b)])
                            P.op("act", lambda e, bw_b=bw_b: e.activation(
                                w_sb[:, :], pb[bw_b][:, :], AF.Copy),
                                reads=[("pb", bw_b)], writes=[("wsb",)])
                        for kc in range(NC):
                            P.op("pe", lambda e, su_i=su_i, kc=kc, hc=hc, t=t, bu=bu: e.matmul(
                                pb[bu][:, :], slab_g(su_i, kc, hc), xns(kc, t),
                                start=(kc == 0), stop=(kc == NC - 1)),
                                reads=[("slab", su_i), ("xn", kc, t)], writes=[("pb", bu)],
                                inc=(kc == NC - 1))
                        P.op("act", lambda e, hc=hc, bg=bg: e.activation(
                            sg[:, hc % 2, :], pb[bg][:, :], AF.Silu),
                            reads=[("pb", bg)], writes=[("sg", hc % 2)])
                        P.op("dve", lambda e, hc=hc: e.tensor_mul(
                            t2f[:, :], sg[:, hc % 2, :], w_sb[:, :]),
                            reads=[("sg", hc % 2), ("wsb",)], writes=[("scr32",)])
                        P.op("dve", lambda e, hc=hc, bu=bu, par=par: e.tensor_mul(
                            h_w[:, par, hc, :], t2f[:, :], pb[bu][:, :]),
                            reads=[("scr32",), ("pb", bu)], writes=[("hw", par, hc)])
                    if pending_down:
                        emit_down(*pending_down.pop())
                        if t == 0 and e_i >= 1:
                            plan_dma_down(e_i)
                    pending_down.append((e_i, t, par))
                if e_i + 2 < n_experts:
                    plan_dma_gup(e_i + 2)
                if tail_dma is not None and e_i in tail_dma:
                    tail_dma[e_i]()
            emit_down(*pending_down.pop())

        def _wo_tile(ec, fc):
            return slab_g(1 if fc < 4 else 2, ec, fc % 4)

        def load_w2(dram, s0, s1):
            P.dma("pool", f"dma_s{s0}", lambda e: e.dma_start(
                slab[s0][:, :].rearrange("p (kc h) -> p kc h", h=512),
                dram[:, 0:512].rearrange("(kc p) h -> p kc h", p=128)),
                writes=[("slab", s0)])
            P.dma("pool", f"dma_s{s1}", lambda e: e.dma_start(
                slab[s1][:, :].rearrange("p (kc h) -> p kc h", h=512),
                dram[:, 512:1024].rearrange("(kc p) h -> p kc h", p=128)),
                writes=[("slab", s1)])

        def attn_dma_a():
            P.dma("pool", "dma_s0", lambda e: e.dma_start(
                slab[0][:, :].rearrange("p (kc h) -> p kc h", h=512),
                WkT[:, 0:512].rearrange("(kc p) h -> p kc h", p=128)),
                writes=[("slab", 0)])
            P.dma("pool", "dma_s2", lambda e: e.dma_start(
                slab[2][:, :].rearrange("p (kc h) -> p kc h", h=512),
                WvT[:, 0:512].rearrange("(kc p) h -> p kc h", p=128)),
                writes=[("slab", 2)])

        def attn_dma_b():
            P.dma("pool", "dma_s1", lambda e: e.dma_start(
                slab[1][:, :].rearrange("p (kc h) -> p kc h", h=512),
                WkT[:, 512:1024].rearrange("(kc p) h -> p kc h", p=128)),
                writes=[("slab", 1)])
            P.dma("pool", "dma_s3", lambda e: e.dma_start(
                slab[3][:, :].rearrange("p (kc h) -> p kc h", h=512),
                WvT[:, 512:1024].rearrange("(kc p) h -> p kc h", p=128)),
                writes=[("slab", 3)])

        def plan_attn(lnf):
            P.dma("pool", "dma_s4", lambda e: e.dma_start(
                slab[4][:, :].rearrange("p (kc h) -> p kc h", h=512),
                WqT[:, 0:512].rearrange("(kc p) h -> p kc h", p=128)),
                writes=[("slab", 4)])

            def proj_fm(sch):
                par = sch % 2
                t = (sch * 128) // TS
                for n in range(2):
                    bK, bV = n, 2 + n
                    for kc in range(NC):
                        P.op("pe", lambda e, kc=kc, sch=sch, n=n, bK=bK: e.matmul(
                            pb[bK][:, :], xn[:, kc, sch * 128:(sch + 1) * 128],
                            slab[n][:, kc * 512:(kc + 1) * 512],
                            start=(kc == 0), stop=False),
                            reads=[("xn", kc, t), ("slab", n)], writes=[("pb", bK)],
                            inc=False)
                    P.op("pe", lambda e, n=n, bK=bK: e.matmul(
                        pb[bK][:, :], ones_bf[:, :], bk_sb[:, n * 512:(n + 1) * 512],
                        start=False, stop=True),
                        reads=[("ones",), ("bk",)], writes=[("pb", bK)])
                    for kc in range(NC):
                        P.op("pe", lambda e, kc=kc, sch=sch, n=n, bV=bV: e.matmul(
                            pb[bV][:, :], xn[:, kc, sch * 128:(sch + 1) * 128],
                            slab[2 + n][:, kc * 512:(kc + 1) * 512],
                            start=(kc == 0), stop=False),
                            reads=[("xn", kc, t), ("slab", 2 + n)], writes=[("pb", bV)],
                            inc=False)
                    P.op("pe", lambda e, n=n, bV=bV: e.matmul(
                        pb[bV][:, :], ones_bf[:, :], bv_sb[:, n * 512:(n + 1) * 512],
                        start=False, stop=True),
                        reads=[("ones",), ("bv",)], writes=[("pb", bV)])
                    P.op("act", lambda e, bK=bK: e.activation(
                        t1[:, 0, :], pb[bK][:, :], AF.Relu),
                        reads=[("pb", bK)], writes=[("t1", 0)])
                    P.op("act", lambda e, bK=bK: e.activation(
                        t1[:, 1, :], pb[bK][:, :], AF.Exp),
                        reads=[("pb", bK)], writes=[("t1", 1)])
                    P.op("dve", lambda e: e.tensor_scalar_min(
                        t1[:, 1, :], t1[:, 1, :], 1.0),
                        reads=[("t1", 1)], writes=[("t1", 1)])
                    P.op("dve", lambda e, n=n, par=par: e.tensor_add(
                        Ksch[:, par, n * 512:(n + 1) * 512], t1[:, 0, :], t1[:, 1, :]),
                        reads=[("t1", 0), ("t1", 1)], writes=[("Ksch", par, n)])
                    P.op("act", lambda e, n=n, par=par, bV=bV: e.activation(
                        bass.AP(Vsch, par * (H * 65) + n * 8 * 65,
                                [[2 * H * 65, 128], [65, 8], [1, 64]]),
                        pb[bV][:, :], AF.Copy),
                        reads=[("pb", bV)], writes=[("Vsch", par, n)])

            def kv_mms(sch):
                par = sch % 2
                for h in range(H):
                    bank = 4 if h < 14 else 5
                    coff = (h // 2) * 65 if h < 14 else 0
                    poff = 64 * (h % 2)
                    P.op("pe", lambda e, h=h, par=par, bank=bank, coff=coff, poff=poff: e.matmul(
                        pb[bank][poff:poff + 64, coff:coff + 65],
                        Ksch[:, par, h * 64:(h + 1) * 64],
                        Vsch[:, par, h * 65:(h + 1) * 65],
                        start=True, stop=True),
                        reads=[("Ksch", par, h // 8), ("Vsch", par, h // 8), ("vones",)],
                        writes=[("pb", bank)],
                        inc=(h in (13, 15)))
                if sch == 0:
                    P.op("dve", lambda e: e.tensor_copy(KVacc[:, 0:455], pb[4][:, 0:455]),
                         reads=[("pb", 4)], writes=[("kvacc",)])
                    P.op("dve", lambda e: e.tensor_copy(KVacc[:, 455:520], pb[5][:, 0:65]),
                         reads=[("pb", 5)], writes=[("kvacc2",)])
                else:
                    P.op("dve", lambda e: e.tensor_add(
                        KVacc[:, 0:455], KVacc[:, 0:455], pb[4][:, 0:455]),
                        reads=[("pb", 4), ("kvacc",)], writes=[("kvacc",)])
                    P.op("dve", lambda e: e.tensor_add(
                        KVacc[:, 455:520], KVacc[:, 455:520], pb[5][:, 0:65]),
                        reads=[("pb", 5), ("kvacc2",)], writes=[("kvacc2",)])

            for sch in range(NSCH):
                if sch % 4 == 0:
                    lnf["bca_mm"](sch // 4)
                    lnf["bca_xn"](sch // 4)
                proj_fm(sch)
                if sch >= 1:
                    kv_mms(sch - 1)
            kv_mms(NSCH - 1)
            P.op("act", lambda e: e.activation(KVbf[:, :], KVacc[:, :], AF.Copy),
                 reads=[("kvacc",), ("kvacc2",)], writes=[("kvbf",)])

            P.dma("pool", "dma_s0", lambda e: e.dma_start(
                slab[0][:, :].rearrange("p (kc h) -> p kc h", h=512),
                WqT[:, 512:1024].rearrange("(kc p) h -> p kc h", p=128)),
                writes=[("slab", 0)])
            P.dma("pool", "dma_s1", lambda e: e.dma_start(
                slab[1][:, :].rearrange("p (kc h) -> p kc h", h=512),
                WoT[:, 0:512].rearrange("(kc p) h -> p kc h", p=128)),
                writes=[("slab", 1)])
            P.dma("pool", "dma_s2", lambda e: e.dma_start(
                slab[2][:, :].rearrange("p (kc h) -> p kc h", h=512),
                WoT[:, 512:1024].rearrange("(kc p) h -> p kc h", p=128)),
                writes=[("slab", 2)])

            def kv_ap(h, c0, cn):
                coff = (h // 2) * 65 if h < 14 else 455
                poff = 64 * (h % 2)
                return KVbf[poff:poff + 64, coff + c0:coff + c0 + cn]

            P.op("pool", lambda e: e.memset(ksmat[:, :, :], 0.0), writes=[("ksmat",)])
            for h in range(H):
                P.op("act", lambda e, h=h: e.activation(
                    ksmat[64 * (h % 2):64 * (h % 2) + 64, h // 2, h:h + 1],
                    kv_ap(h, 64, 1), AF.Copy),
                    reads=[("kvbf",)], writes=[("ksmat",)])

            def qproj(t, dn=False):
                for qc in range(NC):
                    bq_b = qc % 2
                    for kc in range(NC):
                        P.op("pe", lambda e, kc=kc, qc=qc, t=t, bq_b=bq_b: e.matmul(
                            pb[bq_b][:, :], slab_g(4 if qc < 4 else 0, kc, qc % 4), xns(kc, t),
                            start=(kc == 0), stop=(kc == NC - 1)),
                            reads=[("slab", 4 if qc < 4 else 0), ("xn", kc, t)],
                            writes=[("pb", bq_b)], inc=(kc == NC - 1))
                    P.op("act", lambda e, qc=qc, bq_b=bq_b: e.activation(
                        t1[:, 0, :], pb[bq_b][:, :], AF.Relu, bias=bq_sb[:, qc:qc + 1]),
                        reads=[("pb", bq_b), ("bq",)], writes=[("t1", 0)])
                    P.op("act", lambda e, qc=qc, bq_b=bq_b: e.activation(
                        t1[:, 1, :], pb[bq_b][:, :], AF.Exp, bias=bq_sb[:, qc:qc + 1]),
                        reads=[("pb", bq_b), ("bq",)], writes=[("t1", 1)])
                    P.op("dve", lambda e: e.tensor_scalar_min(
                        t1[:, 1, :], t1[:, 1, :], 1.0),
                        reads=[("t1", 1)], writes=[("t1", 1)])
                    bias_w = [("bk",)] if qc < 2 else ([("bv",)] if qc < 4 else [])
                    P.op("dve", lambda e, qc=qc: e.tensor_add(
                        Qp[:, qc, :], t1[:, 0, :], t1[:, 1, :]),
                        reads=[("t1", 0), ("t1", 1)],
                        writes=[("Qp", qc), ("stripe", qc)] + bias_w)
                    if dn and qc >= 1:
                        dn_mm(qc - 1)
                if dn:
                    dn_mm(NC - 1)

            def dn_mm(kc):
                P.op("pe", lambda e, kc=kc: e.matmul(
                    pb[6][0:16, :], ksmat[:, kc, :], Qp[:, kc, :],
                    start=(kc == 0), stop=(kc == NC - 1)),
                    reads=[("ksmat",), ("Qp", kc)], writes=[("pb", 6)],
                    inc=(kc == NC - 1))

            def recip_chain():
                P.op("dve", lambda e: e.tensor_scalar_add(
                    scr32[0:16, :], pb[6][0:16, :], Z_EPS),
                    reads=[("pb", 6)], writes=[("scr32",)])
                P.op("dve", lambda e: e.reciprocal(scr32[0:16, :], scr32[0:16, :]),
                     reads=[("scr32",)], writes=[("scr32",)])
                P.op("act", lambda e: e.activation(rbf[0:16, :], scr32[0:16, :], AF.Copy),
                     reads=[("scr32",)], writes=[("rbf",)])

            def qrv(kc):
                return Ksch[:, 0, (kc % 2) * TS:(kc % 2) * TS + TS]

            def rb_num_wo(t):
                def rb_mm(kc):
                    br = 6 + kc % 2
                    P.op("pe", lambda e, kc=kc, br=br: e.matmul(
                        pb[br][:, :], esel[0:16, kc * 128:(kc + 1) * 128],
                        rbf[0:16, :], start=True, stop=True),
                        reads=[("esel",), ("rbf",)], writes=[("pb", br)])
                rb_mm(0)
                for kc in range(NC):
                    if kc + 1 < NC:
                        rb_mm(kc + 1)
                    br = 6 + kc % 2
                    P.op("dve", lambda e, kc=kc, br=br: e.tensor_mul(
                        qrv(kc), Qp[:, kc, :], pb[br][:, :]),
                        reads=[("Qp", kc), ("pb", br)], writes=[("qr", kc % 2)])
                    bn = 2 + kc % 2
                    for j in range(2):
                        h = 2 * kc + j
                        P.op("pe", lambda e, h=h, j=j, kc=kc, bn=bn: e.matmul(
                            pb[bn][64 * j:64 * j + 64, :], kv_ap(h, 0, 64),
                            qrv(kc)[64 * j:64 * j + 64, :], start=True, stop=True),
                            reads=[("kvbf",), ("qr", kc % 2)], writes=[("pb", bn)],
                            inc=(j == 1))
                    P.op("act", lambda e, kc=kc, bn=bn: e.activation(
                        stripe[:, kc, :], pb[bn][:, :], AF.Copy),
                        reads=[("pb", bn)], writes=[("stripe", kc), ("Qp", kc)])
                for fc in range(NC):
                    bw = 4 + fc % 2
                    for ec in range(NC):
                        P.op("pe", lambda e, fc=fc, ec=ec, bw=bw: e.matmul(
                            pb[bw][:, :], _wo_tile(ec, fc), stripe[:, ec, :],
                            start=(ec == 0), stop=(ec == NC - 1)),
                            reads=[("slab", 1), ("slab", 2), ("stripe", ec)],
                            writes=[("pb", bw)], inc=(ec == NC - 1))
                    P.op("dve", lambda e, fc=fc, t=t, bw=bw: e.scalar_tensor_tensor(
                        xs(fc, t), pb[bw][:, :], bo_sb[:, fc:fc + 1], xs(fc, t),
                        op0=ALU.add, op1=ALU.add),
                        reads=[("pb", bw), ("bo",), ("x", fc, t)],
                        writes=[("x", fc, t)])

            for t in range(NT):
                qproj(t, dn=True)
                recip_chain()
                rb_num_wo(t)

        if do_f1:
            plan_ffn("f1", plan_ln("f1"), plan_gates("f1"),
                     tail_dma={NE_ALL - 2: attn_dma_a, NE_ALL - 1: attn_dma_b} if do_attn else None)
        if do_attn:
            plan_attn(plan_ln("attn"))
        if do_f2:
            plan_ffn("f2", plan_ln("f2"), plan_gates("f2"))

        for t in range(NT):
            P.dma("sp", f"dma_x{t}", lambda e, t=t: e.dma_start(
                out_T[:, t * TS:(t + 1) * TS].rearrange("(c p) t -> p c t", p=128),
                x_sb[:, :, t * TS:(t + 1) * TS]),
                reads=[("x", fc, t) for fc in range(NC)])
        P.finish()
        P.replay(block, sems)
    return nc



def _stage(inputs):
    f = np.float32

    def T2(a):
        return np.ascontiguousarray(np.asarray(a).astype(f).T)

    Wg = np.asarray(inputs["L0_ffn_Wg"], dtype=f)
    Wsg = np.asarray(inputs["L0_ffn_Wsg"], dtype=f)
    wg_cat = np.ascontiguousarray(np.concatenate([Wg, Wsg], 0).T)
    shared = {
        "WqT": T2(inputs["L0_attn_Wq"]),
        "WkT": T2(inputs["L0_attn_Wk"]),
        "WvT": T2(inputs["L0_attn_Wv"]),
        "WoT": T2(inputs["L0_attn_Wo"]),
        "bq_c": np.ascontiguousarray(np.asarray(inputs["L0_attn_bq"], dtype=f).reshape(NC, 128).T),
        "bo_c": np.ascontiguousarray(np.asarray(inputs["L0_attn_bo"], dtype=f).reshape(NC, 128).T),
        "bk_r": np.asarray(inputs["L0_attn_bk"], dtype=f).reshape(1, D).copy(),
        "bv_r": np.asarray(inputs["L0_attn_bv"], dtype=f).reshape(1, D).copy(),
        "WgT": wg_cat,
        "WegT": np.ascontiguousarray(np.asarray(inputs["L0_ffn_We_gate"], dtype=f).transpose(0, 2, 1)),
        "WeuT": np.ascontiguousarray(np.asarray(inputs["L0_ffn_We_up"], dtype=f).transpose(0, 2, 1)),
        "WedT": np.ascontiguousarray(np.asarray(inputs["L0_ffn_We_down"], dtype=f).transpose(0, 2, 1)),
        "WsgT": T2(inputs["L0_ffn_Ws_gate"]),
        "WsuT": T2(inputs["L0_ffn_Ws_up"]),
        "WsdT": T2(inputs["L0_ffn_Ws_down"]),
    }
    x = np.asarray(inputs["L0_attn_x"], dtype=f)
    in_maps = []
    for c in range(8):
        m = dict(shared)
        m["x_T"] = np.ascontiguousarray(x[c].T)
        in_maps.append(m)
    return in_maps


def kernel(**inputs):
    in_maps = _stage(inputs)
    nc = build()
    res = run_bass_kernel_spmd(nc, in_maps, core_ids=list(range(8)))
    out = np.stack([np.asarray(r["out_T"]).T for r in res.results], axis=0)
    return np.ascontiguousarray(out.astype(np.float32))




